# version 09
# baseline (speedup 1.0000x reference)
import contextlib
import numpy as np
import concourse.bass as bass
import concourse.mybir as mybir
from concourse.bass_utils import run_bass_kernel_spmd
import concourse.dve_ops as dve_ops
from concourse.dve_op import DveOp
from concourse.dve_spec import Spec, Src0, Src1, C0, C1, C2, maxx, minn
from concourse.dve_ops_raw import _compute_dtype

F32 = mybir.dt.float32
BF16 = mybir.dt.bfloat16
AF = mybir.ActivationFunctionType
ALU = mybir.AluOpType
AX = mybir.AxisListType
MAGIC = 12582912.0

D = 1024
S = 2048
H = 4096
DC = 8
HC = 32
TB = 128
NBLK = S // TB
NPAIR = NBLK // 2
TA = 256
NA = S // TA
QW = 2080
QOFF = 16
EPS = 1e-5


def _ref_rmc(in0, in1, s0, s1, imm2):
    f = _compute_dtype(in0)
    return np.clip(np.round(in0.astype(f) * in1.astype(f)), s0, s1).astype(f)


def _ref_qs(in0, in1, s0, s1, imm2):
    f = _compute_dtype(in0)
    return (np.round(in0.astype(f) * s0) * s1).astype(f)


def _sha_of(body, ref):
    probe = DveOp("TMP_SHA", Spec(body=body), subdim=False, uops_sha={},
                  reference=ref)
    try:
        probe.compile("v3")
    except ValueError as exc:
        msg = str(exc)
        key = 'uops_sha["v3"]="'
        i = msg.index(key) + len(key)
        return msg[i:msg.index('"', i)]
    raise RuntimeError("expected sha mismatch error")


def _reg(name, body, ref):
    for op in dve_ops.OPS:
        if op.name == name:
            return op
    op = DveOp(name, Spec(body=body), subdim=False,
               uops_sha={"v3": _sha_of(body, ref)}, reference=ref)
    dve_ops.OPS.append(op)
    dve_ops.CUSTOM_DVE_SPECS[op.name] = op.spec
    dve_ops.CUSTOM_DVE_REFS[op.name] = op.reference
    row = max(dve_ops._SUB_OPCODE_FOR_NAME.values()) + 1
    assert row < 0x20, "no custom-DVE opcode row left"
    dve_ops._SUB_OPCODE_FOR_NAME[op.name] = row
    return op


def _ref_rm(in0, in1, s0, s1, imm2):
    f = _compute_dtype(in0)
    return np.round(in0.astype(f) * in1.astype(f)).astype(f)


def make_ops():
    op_rmc = _reg("ROUND_MUL_CLIP",
                  minn(maxx((Src0 * Src1 + C2) - C2, C0), C1), _ref_rmc)
    op_rm = _reg("ROUND_MUL", (Src0 * Src1 + C0) - C0, _ref_rm)
    return op_rmc, None, op_rm


class Sems:

    def __init__(self, nc, es):
        self.nc = nc
        self.es = es
        self.h = {}
        self.incs = {}
        self.maxw = {}

    def _get(self, name):
        if name not in self.h:
            self.h[name] = self.es.enter_context(self.nc.semaphore(name))
            self.incs[name] = 0
            self.maxw[name] = 0
        return self.h[name]

    def inc(self, instr, name, amount=1):
        instr.then_inc(self._get(name), amount)
        self.incs[name] += amount

    def wait(self, eng, name, value):
        if value <= 0:
            self._get(name)
            return
        sem = self._get(name)
        self.maxw[name] = max(self.maxw[name], value)
        eng.wait_ge(sem, value)

    def check(self):
        bad = [(k, self.maxw[k], self.incs[k]) for k in self.h
               if self.maxw[k] > self.incs[k]]
        assert not bad, f"unsatisfiable semaphore waits: {bad}"


def build_nc(op_rmc, op_qs, op_rm, dbg=False):
    nc = bass.Bass("TRN2", target_bir_lowering=False,
                   detect_race_conditions=False)
    x_in = nc.dram_tensor("x", [S, D], F32, kind="ExternalInput").ap()
    wk = nc.dram_tensor("wk", [D, D], F32, kind="ExternalInput").ap()
    wv = nc.dram_tensor("wv", [D, D], F32, kind="ExternalInput").ap()
    wr = nc.dram_tensor("wr", [D, D], F32, kind="ExternalInput").ap()
    wo = nc.dram_tensor("wo", [D, D], F32, kind="ExternalInput").ap()
    td = nc.dram_tensor("td", [D], F32, kind="ExternalInput").ap()
    tf = nc.dram_tensor("tf", [D], F32, kind="ExternalInput").ap()
    w1 = nc.dram_tensor("w1", [H, D], F32, kind="ExternalInput").ap()
    cw = nc.dram_tensor("cw", [H, 1, 3], F32, kind="ExternalInput").ap()
    cb = nc.dram_tensor("cb", [H], F32, kind="ExternalInput").ap()
    w2 = nc.dram_tensor("w2", [D, H], F32, kind="ExternalInput").ap()
    out = nc.dram_tensor("out", [S, D], F32, kind="ExternalOutput").ap()
    _ik = "ExternalOutput" if dbg else "Internal"
    resid1 = nc.dram_tensor("resid1", [S, D], F32, kind=_ik).ap()
    resid2 = nc.dram_tensor("resid2", [S, D], F32, kind=_ik).ap()
    w1q_st = nc.dram_tensor("w1q_st", [128, DC * H], BF16, kind=_ik).ap()
    w2q_st = nc.dram_tensor("w2q_st", [128, HC * D], BF16, kind=_ik).ap()
    hq_dram = nc.dram_tensor("hq_dram", [128, NBLK * H], BF16,
                             kind=_ik).ap()

    with contextlib.ExitStack() as es:
        e = es.enter_context
        ident = e(nc.sbuf_tensor("ident", [128, 128], F32))
        ones128 = e(nc.sbuf_tensor("ones128", [128, 128], F32))
        cw_c = e(nc.sbuf_tensor("cw_c", [128, 96], F32))
        cwm = e(nc.sbuf_tensor("cwm", [128, 96], F32))
        cb_c = e(nc.sbuf_tensor("cb_c", [128, 32], F32))
        td_c = e(nc.sbuf_tensor("td_c", [128, 8], F32))
        tf_c = e(nc.sbuf_tensor("tf_c", [128, 8], F32))
        decay_c = e(nc.sbuf_tensor("decay_c", [128, 8], F32))
        eu_c = e(nc.sbuf_tensor("eu_c", [128, 8], F32))
        ccol = e(nc.sbuf_tensor("ccol", [128, 36], F32))
        sc2st = e(nc.sbuf_tensor("sc2st", [128, NBLK], F32))
        s2row = e(nc.sbuf_tensor("s2row", [1, 128], F32))
        bn12 = e(nc.sbuf_tensor("bn12", [128, 12], F32))
        msq4 = e(nc.sbuf_tensor("msq4", [128, 4], F32))
        jnk = e(nc.sbuf_tensor("jnk", [128, 4], F32))
        jnkb = e(nc.sbuf_tensor("jnkb", [128, 128], BF16))
        AB = e(nc.sbuf_tensor("AB", [128, 91392], BF16,
                              is_transpose_dest=True))
        AF32 = e(nc.sbuf_tensor("AF", [128, 6160], F32))
        slotA = AB[:, 0:32768]
        slotB = AB[:, 32768:65536]
        slotC_off = 65536
        silu_off = 82176
        qs_off = 90368
        psum = [e(nc.psum_tensor(f"ps{i}", [128, 512], F32)) for i in range(8)]
        epsc = ccol[:, 30:31]
        nhc = ccol[:, 31:32]
        nc.const_aps.register(F32, EPS, epsc)
        nc.const_aps.register(F32, -0.5, nhc)
        nc.const_aps.register(F32, MAGIC, ccol[:, 34:35])
        nc.const_aps.register(F32, -MAGIC, ccol[:, 35:36])
        sems = Sems(nc, es)
        block = e(nc.Block())

        colnames = ["acc", "mean1c", "m1inv", "mean2c", "m2inv", "m2h",
                    "m", "var", "rstd", "nmr", "mx", "mxc", "rc", "s1",
                    "inv1", "mxh", "i2", "s2", "tmp", "tmp2", "e1a",
                    "stmp"]
        CCI = {k: i for i, k in enumerate(colnames)}
        CCI["acc2"] = 32
        CCI["tmpb"] = 33

        def emit(role, eng):
            def on(r):
                return role == r

            C = {k: ccol[:, i:i + 1] for k, i in CCI.items()}
            e1cols = ccol[:, 22:30]

            def join(tag, pre=()):
                nm = "join_" + tag
                for snm, val in pre:
                    sems.wait(eng, snm, val)
                if on("vector"):
                    op = eng.memset(jnk[:, 0:1], 0.0)
                elif on("scalar"):
                    op = eng.activation(jnk[:, 1:2], ccol[:, 30:31], AF.Copy,
                                        bias=0.0, scale=1.0)
                elif on("gpsimd"):
                    op = eng.memset(jnk[:, 2:3], 0.0)
                elif on("tensor"):
                    op = eng.ldweights(jnkb[:, 0:128])
                else:
                    op = eng.nop()
                sems.inc(op, nm, 1)
                sems.wait(eng, nm, 5)

            def af(off, n):
                return AF32[:, off:off + n]

            if on("vector"):
                eng.memset(ident[:], 0.0)
                eng.memset(ones128[:], 1.0)
                eng.memset(jnk[:], 0.0)
                eng.memset(jnkb[:], 0.0)
                eng.memset(ccol[:, 30:31], EPS)
                eng.memset(ccol[:, 34:35], MAGIC)
                eng.memset(ccol[:, 35:36], -MAGIC)
                op = eng.memset(ccol[:, 31:32], -0.5)
                sems.inc(op, "v_memset", 1)
            if on("gpsimd"):
                sems.wait(eng, "v_memset", 1)
                op = eng.affine_select(out=ident[:], in_=ident[:],
                                        compare_op=ALU.not_equal, fill=1.0,
                                        base=0, pattern=[[-1, 128]],
                                        channel_multiplier=1)
                sems.inc(op, "g_const", 1)

            if on("sync"):
                d1 = eng.dma_start(
                    out=cw_c[:, 0:96].rearrange("p (c k) -> p c k", c=32),
                    in_=cw.rearrange("(c p) one k -> p c (one k)", p=128))
                sems.inc(d1, "p_small", 16)
                d2 = eng.dma_start(out=cb_c[:, 0:32],
                                   in_=cb.rearrange("(c p) -> p c", p=128),
                                   allow_slow_non_contiguous=True)
                sems.inc(d2, "p_small", 16)
                d3 = eng.dma_start(out=td_c[:, 0:8],
                                   in_=td.rearrange("(c p) -> p c", p=128),
                                   allow_slow_non_contiguous=True)
                sems.inc(d3, "p_small", 16)
                d4 = eng.dma_start(out=tf_c[:, 0:8],
                                   in_=tf.rearrange("(c p) -> p c", p=128),
                                   allow_slow_non_contiguous=True)
                sems.inc(d4, "p_small", 16)

            if on("scalar"):
                sems.wait(eng, "p_small", 64)
                op = eng.activation(e1cols, td_c[:], AF.Exp)
                sems.inc(op, "p_exp1", 1)
                sems.wait(eng, "p_neg", 1)
                eng.activation(decay_c[:], e1cols, AF.Exp)
                op = eng.activation(eu_c[:], tf_c[:], AF.Exp)
                sems.inc(op, "p_exp2", 1)
            if on("vector"):
                sems.wait(eng, "p_exp1", 1)
                op = eng.tensor_scalar(e1cols, e1cols, -1.0, None, ALU.mult)
                sems.inc(op, "p_neg", 1)

            w1stg = [af(0 + b * D, D) for b in range(2)]
            w2stg = [af(2048 + b * 2048, 2048) for b in range(2)]
            w1rt = [AB[:, slotC_off + b * D:slotC_off + (b + 1) * D]
                    for b in range(2)]
            w2rt = [AB[:, slotC_off + 2048 + b * 2048:
                       slotC_off + 2048 + (b + 1) * 2048] for b in range(2)]

            def w2src(j):
                return w2[128 * (j // 2):128 * (j // 2) + 128,
                          (j % 2) * 2048:(j % 2 + 1) * 2048]

            if on("gpsimd"):
                for i in range(32):
                    sems.wait(eng, "pw1s", i - 1)
                    d = eng.dma_start(out=w1stg[i % 2][:],
                                      in_=w1[128 * i:128 * (i + 1), :])
                    sems.inc(d, "pw1a" + str(i % 2), 16)
                for i in range(32):
                    sems.wait(eng, "pw1s", (31 + i) if i < 2 else 0)
                    sems.wait(eng, "pw1r", i - 1)
                    d = eng.dma_start(out=w1stg[i % 2][:],
                                      in_=w1[128 * i:128 * (i + 1), :])
                    sems.inc(d, "pw1a" + str(i % 2), 16)
            def w2_tr(eng, j):
                sems.wait(eng, "pw2r", j + 1)
                d = eng.dma_start_transpose(
                    slotB.rearrange("p (c n) -> p c n", c=HC)
                    [:, 16 * (j % 2):16 * (j % 2) + 16,
                     128 * (j // 2):128 * (j // 2) + 128],
                    w2rt[j % 2][:])
                sems.inc(d, "pw2t" + str(j % 2), 16)

            if on("scalar"):
                for j in range(16):
                    sems.wait(eng, "pw2s", j - 1)
                    d = eng.dma_start(out=w2stg[j % 2][:], in_=w2src(j))
                    sems.inc(d, "pw2a" + str(j % 2), 16)
                for j in range(16):
                    sems.wait(eng, "pw2s", (15 + j) if j < 2 else 0)
                    sems.wait(eng, "pw2r", j - 1)
                    d = eng.dma_start(out=w2stg[j % 2][:], in_=w2src(j))
                    sems.inc(d, "pw2a" + str(j % 2), 16)
            if on("tensor"):
                sems.wait(eng, "g_const", 1)
                sems.wait(eng, "pw1s", 32)
                mm = eng.matmul(psum[7][:, 0:1], ones128[:], C["acc"],
                                start=True, stop=True)
                sems.inc(mm, "pw1pe", 1)
                sems.wait(eng, "pw2s", 16)
                mm = eng.matmul(psum[7][:, 1:2], ones128[:], C["acc2"],
                                start=True, stop=True)
                sems.inc(mm, "pw2pe", 1)
            if on("vector"):
                eng.memset(C["acc"], 0.0)
                eng.memset(C["acc2"], 0.0)
                for i in range(32):
                    sems.wait(eng, "pw1a" + str(i % 2), 16 * (i // 2 + 1))
                    eng.tensor_reduce(C["tmp"], w1stg[i % 2][:], axis=AX.X,
                                      op=ALU.add, apply_absolute_value=True)
                    eng.drain()
                    op = eng.tensor_add(C["acc"], C["acc"], C["tmp"])
                    sems.inc(op, "pw1s", 1)
                    if i % 2 == 1:
                        j = i // 2
                        sems.wait(eng, "pw2a" + str(j % 2),
                                  16 * (j // 2 + 1))
                        eng.tensor_reduce(C["tmpb"], w2stg[j % 2][:],
                                          axis=AX.X, op=ALU.add,
                                          apply_absolute_value=True)
                        eng.drain()
                        op = eng.tensor_add(C["acc2"], C["acc2"], C["tmpb"])
                        sems.inc(op, "pw2s", 1)
                sems.wait(eng, "pw1pe", 1)
                eng.tensor_scalar(C["mean1c"], psum[7][:, 0:1],
                                  1.0 / (32 * 128 * D), 1e-5,
                                  ALU.mult, ALU.max)
                eng.drain()
                eng.reciprocal(C["m1inv"], C["mean1c"])
                eng.drain()
                sems.wait(eng, "pw2pe", 1)
                eng.tensor_scalar(C["mean2c"], psum[7][:, 1:2],
                                  1.0 / (16 * 128 * 2048), 1e-5,
                                  ALU.mult, ALU.max)
                eng.drain()
                eng.reciprocal(C["m2inv"], C["mean2c"])
                eng.drain()
                for i in range(32):
                    sems.wait(eng, "pw1a" + str(i % 2),
                              16 * (16 + i // 2 + 1))
                    sems.wait(eng, "pw1t" + str(i % 2), 16 * (i // 2))
                    op = eng._custom_dve(
                        op_rmc, out=w1rt[i % 2][:], in0=w1stg[i % 2][:],
                        in1=ccol[:, CCI["m1inv"]:CCI["m1inv"] + 1]
                        .broadcast_to([128, D]),
                        s0=-1.0, s1=1.0, imm2=MAGIC)
                    sems.inc(op, "pw1r", 1)
                    if i % 2 == 1:
                        j = i // 2
                        sems.wait(eng, "pw2a" + str(j % 2),
                                  16 * (8 + j // 2 + 1))
                        sems.wait(eng, "pw2t" + str(j % 2), 16 * (j // 2))
                        op = eng._custom_dve(
                            op_rmc, out=w2rt[j % 2][:], in0=w2stg[j % 2][:],
                            in1=ccol[:, CCI["m2inv"]:CCI["m2inv"] + 1]
                            .broadcast_to([128, 2048]),
                            s0=-1.0, s1=1.0, imm2=MAGIC)
                        sems.inc(op, "pw2r", 1)
            if on("sync"):
                for i in range(32):
                    sems.wait(eng, "pw1r", i + 1)
                    d = eng.dma_start_transpose(
                        slotA.rearrange("p (c n) -> p c n", c=DC)
                        [:, :, 128 * i:128 * (i + 1)],
                        w1rt[i % 2][:])
                    sems.inc(d, "pw1t" + str(i % 2), 16)
                    if i % 2 == 1:
                        w2_tr(eng, i // 2)
            if on("vector"):
                sems.wait(eng, "p_small", 64)
                eng.tensor_scalar(cwm[:], cw_c[:], C["mean1c"], None,
                                  ALU.mult)
                eng.tensor_scalar(C["m2h"], C["mean2c"], 0.5, None,
                                  ALU.mult)
            if on("scalar"):
                sems.wait(eng, "pw1t0", 16 * 16)
                sems.wait(eng, "pw1t1", 16 * 16)
                d = eng.dma_start(out=w1q_st[:, :], in_=slotA)
                sems.inc(d, "p_st1", 16)
                sems.wait(eng, "pw2t0", 16 * 8)
                sems.wait(eng, "pw2t1", 16 * 8)
                d = eng.dma_start(out=w2q_st[:, :], in_=slotB)
                sems.inc(d, "p_st2", 16)

            prep_pre = []
            if on("sync"):
                prep_pre = [("pw1t0", 16 * 16), ("pw1t1", 16 * 16),
                            ("p_small", 64)]
            if on("scalar"):
                prep_pre = [("pw2a0", 16 * 16), ("pw2a1", 16 * 16),
                            ("pw2t0", 16 * 8), ("pw2t1", 16 * 8)]
            if on("gpsimd"):
                prep_pre = [("pw1a0", 16 * 32), ("pw1a1", 16 * 32)]
            join("prep", prep_pre)

            def ln_stats_v(x_):
                eng.bn_stats(bn12[:, 0:6], x_[:, 0:512])
                eng.bn_stats(bn12[:, 6:12], x_[:, 512:1024])
                eng.drain()
                m3 = bn12.rearrange("p (g three) -> p g three", three=3)
                t3 = bn12.rearrange("p (g three) -> p three g", three=3)
                eng.tensor_reduce(C["m"], t3[:, 1:2, :], axis=AX.X,
                                  op=ALU.add)
                eng.tensor_reduce(C["var"], t3[:, 2:3, :], axis=AX.X,
                                  op=ALU.add)
                eng.tensor_mul(msq4[:], m3[:, :, 1:2], m3[:, :, 1:2])
                eng.drain()
                eng.tensor_reduce(C["tmp"], msq4[:], axis=AX.X, op=ALU.add)
                eng.tensor_scalar(C["m"], C["m"], 0.25, None, ALU.mult)
                eng.drain()
                eng.tensor_scalar(C["tmp"], C["tmp"], 0.25, None, ALU.mult)
                eng.tensor_mul(C["tmp2"], C["m"], C["m"])
                eng.drain()
                eng.scalar_tensor_tensor(C["var"], C["var"], 1.0 / 1024.0,
                                         C["tmp"], ALU.mult, ALU.add)
                eng.drain()
                return eng.tensor_sub(C["var"], C["var"], C["tmp2"])

            def rstd_s():
                eng.activation(C["stmp"], C["var"], AF.Ln, bias=EPS,
                               scale=1.0)
                return eng.activation(C["rstd"], C["stmp"], AF.Exp, bias=0.0,
                                      scale=-0.5)

            def ffn(f):
                t = "F"
                oI = f * NBLK
                oP = f * NPAIR
                oC = f * NPAIR * HC
                rin = x_in if f == 0 else resid2
                rout = resid1 if f == 0 else out
                W1b = slotA if f == 0 else slotB
                W2b = slotB if f == 0 else slotA
                qT = AB[:, slotC_off:slotC_off + DC * QW]
                qTv = qT.rearrange("p (c n) -> p c n", c=DC)
                silu2 = [AB[:, silu_off:silu_off + 2 * H], W2b[:, 0:2 * H]]
                ysc = [af(128 * 2 * m, 256) for m in range(2)]
                xb = af(4096, D)
                xn = af(5120, D)
                Mb = af(3584, 128)
                qsb = AB[:, qs_off:qs_off + D]
                hqB = [AB[:, slotC_off:slotC_off + 2 * H],
                       AB[:, silu_off:silu_off + 2 * H]]

                if on("vector"):
                    eng.memset(qTv[:, :, QOFF - 1:QOFF], 0.0)
                    op = eng.memset(qTv[:, :, QOFF + S:QOFF + S + 1], 0.0)
                    sems.inc(op, t + "hz", 1)

                def xside_v(i):
                    gi = oI + i
                    sems.wait(eng, t + "x", 16 * (gi + 1))
                    op = ln_stats_v(xb)
                    sems.inc(op, t + "st", 1)
                    sems.wait(eng, t + "rs", gi + 1)
                    eng.scalar_tensor_tensor(C["nmr"], C["m"], -1.0,
                                             C["rstd"], ALU.mult, ALU.mult)
                    eng.drain()
                    eng.tensor_scalar(xn[:], xb[:], C["rstd"],
                                      C["nmr"], ALU.mult, ALU.add)
                    eng.reduce_max(C["mx"], xn[:], axis=AX.X,
                                   apply_absolute_value=True)
                    eng.drain()
                    eng.tensor_scalar(C["inv1"], C["mx"], 1e-5, 1.0 / 127.0,
                                      ALU.max, ALU.mult)
                    eng.drain()
                    op = eng.reciprocal(C["s1"], C["inv1"])
                    sems.inc(op, t + "xq", 1)

                def xside_s(i):
                    gi = oI + i
                    sems.wait(eng, t + "st", gi + 1)
                    op = rstd_s()
                    sems.inc(op, t + "rs", 1)
                    sems.wait(eng, t + "xq", gi + 1)
                    eng.activation(xb[:], xn[:], AF.Identity,
                                   bias=MAGIC, scale=C["s1"])
                    eng.activation(xb[:], xb[:], AF.Identity,
                                   bias=-MAGIC, scale=1.0)
                    sems.wait(eng, t + "qt", 16 * gi)
                    op = eng.activation(qsb[:], xb[:], AF.Identity,
                                        bias=0.0, scale=C["inv1"])
                    sems.inc(op, t + "q", 1)

                if on("sync"):
                    for i in range(NBLK):
                        gi = oI + i
                        sems.wait(eng, t + "q", gi)
                        d = eng.dma_start(out=xb[:],
                                          in_=rin[128 * i:128 * (i + 1), :])
                        sems.inc(d, t + "x", 16)
                        sems.wait(eng, t + "q", gi + 1)
                        d = eng.dma_start_transpose(
                            qTv[:, :, QOFF + 128 * i:QOFF + 128 * (i + 1)],
                            qsb[:])
                        sems.inc(d, t + "qt", 16)

                for p in range(NPAIR):
                    tiles = list(range(4)) if p == 0 else (
                        [2 * p + 2, 2 * p + 3] if 2 * p + 3 < NBLK else [])
                    for i in tiles:
                        if on("vector"):
                            xside_v(i)
                        if on("scalar"):
                            xside_s(i)
                    silu = silu2[p % 2]
                    siluv = silu.rearrange("p (c n) -> p c n", c=HC)
                    for c in range(HC):
                        npc = oC + p * HC + c
                        if on("tensor"):
                            if p == 0 and c == 0:
                                sems.wait(eng, t + "hz", f + 1)
                                if f == 1:
                                    sems.wait(eng, "at_w1p", 16)
                            sems.wait(eng, t + "qt",
                                      16 * (oI + min(2 * p + 3, NBLK)))
                            sems.wait(eng, t + "p1", npc - 3)
                            sems.wait(eng, t + "cv", npc - 3)
                            pa = psum[c % 4]
                            for k in range(DC):
                                lhsT = W1b[:, k * H + c * 128:
                                           k * H + (c + 1) * 128]
                                base = k * QW + QOFF + 256 * p
                                mm = eng.matmul(pa[:, 0:258], lhsT,
                                                qT[:, base - 1:base + 257],
                                                start=(k == 0),
                                                stop=(k == DC - 1))
                            sems.inc(mm, t + "mm1", 1)
                        ps = psum[c % 4]
                        ssl = silu[:, c * 256:(c + 1) * 256]
                        yv = ysc[c % 2]
                        if on("scalar"):
                            if c == 0:
                                if p == 1 and f == 0:
                                    sems.wait(eng, "p_st2", 16)
                                sems.wait(eng, t + "hqw" + str(p % 2),
                                          16 * (f * 4 + p // 2))
                            sems.wait(eng, t + "cv", npc - 1)
                            sems.wait(eng, t + "mm1", npc + 1)
                            op = eng.activation(
                                yv[:], ps[:, 1:257], AF.Identity,
                                bias=cb_c[:, c:c + 1],
                                scale=cwm[:, 3 * c + 1:3 * c + 2])
                            sems.inc(op, t + "p1", 1)
                        if on("vector"):
                            sems.wait(eng, t + "p1", npc + 1)
                            eng.scalar_tensor_tensor(
                                yv[:], ps[:, 0:256],
                                cwm[:, 3 * c:3 * c + 1], yv[:],
                                ALU.mult, ALU.add)
                            eng.drain()
                            s2i = eng.scalar_tensor_tensor(
                                ssl, ps[:, 2:258],
                                cwm[:, 3 * c + 2:3 * c + 3], yv[:],
                                ALU.mult, ALU.add)
                            sems.inc(s2i, t + "cv", 1)
                    gp = oP + p
                    if on("scalar"):
                        sems.wait(eng, t + "cv", oC + p * HC + HC // 2)
                        op = eng.activation(silu[:, 0:H], silu[:, 0:H],
                                            AF.Silu)
                        sems.inc(op, t + "sl", 1)
                        sems.wait(eng, t + "cv", oC + (p + 1) * HC)
                        op = eng.activation(silu[:, H:2 * H],
                                            silu[:, H:2 * H], AF.Silu)
                        sems.inc(op, t + "sl", 1)
                    for b in range(2):
                        j = 2 * p + b
                        gj = oI + j
                        bview = siluv[:, :, b * 128:(b + 1) * 128]
                        if on("vector"):
                            sems.wait(eng, t + "sl", 2 * gp + 2)
                            op = eng.reduce_max(
                                Mb.rearrange("p (n one) -> p n one", one=1),
                                bview.rearrange("p c n -> p n c"),
                                axis=AX.X, apply_absolute_value=True)
                            sems.inc(op, t + "mx1", 1)
                        if on("tensor"):
                            sems.wait(eng, t + "mx1", gj + 1)
                            sems.wait(eng, t + "mx2", gj)
                            tr = eng.transpose(psum[4][:, 0:128], Mb[:],
                                               ident[:])
                            sems.inc(tr, t + "mxT", 1)
                        if on("vector"):
                            sems.wait(eng, t + "mxT", gj + 1)
                            eng.reduce_max(C["mxh"], psum[4][:, 0:128],
                                           axis=AX.X)
                            eng.drain()
                            eng.tensor_scalar(C["i2"], C["mxh"], 1e-5,
                                              1.0 / 127.0, ALU.max, ALU.mult)
                            eng.drain()
                            eng.reciprocal(C["s2"], C["i2"])
                            op = eng.tensor_mul(sc2st[:, j:j + 1], C["i2"],
                                                C["m2h"])
                            sems.inc(op, t + "mx2", 1)
                        if on("tensor"):
                            sems.wait(eng, t + "mx2", gj + 1)
                            sems.wait(eng, t + "s2r", gj)
                            tr = eng.transpose(psum[5][0:1, 0:128], C["s2"],
                                               ident[:])
                            sems.inc(tr, t + "s2T", 1)
                        if on("vector"):
                            sems.wait(eng, t + "s2T", gj + 1)
                            op = eng.tensor_copy(s2row[:],
                                                 psum[5][0:1, 0:128])
                            sems.inc(op, t + "s2r", 1)
                        if on("tensor"):
                            sems.wait(eng, t + "s2r", gj + 1)
                            sems.wait(eng, t + "rq", gp)
                            mm = eng.matmul(
                                psum[6][:, b * 128:(b + 1) * 128],
                                ones128[0:1, 0:128],
                                s2row[:], start=True, stop=True)
                            sems.inc(mm, t + "s2b", 1)
                    if on("vector"):
                        sems.wait(eng, t + "s2b", 2 * gp + 2)
                        op = eng._custom_dve(
                            op_rm,
                            out=siluv[:, :, :],
                            in0=siluv[:, :, :],
                            in1=psum[6][:, 0:256]
                            .rearrange("p (c n) -> p c n", c=1)
                            .broadcast_to([128, HC, 256]),
                            s0=MAGIC, s1=0.0, imm2=0.0)
                        sems.inc(op, t + "rq", 1)
                    if on("scalar"):
                        sems.wait(eng, t + "rq", gp + 1)
                        d = eng.dma_start(
                            out=hq_dram[:, p * 2 * H:(p + 1) * 2 * H],
                            in_=silu[:])
                        sems.inc(d, t + "hqw" + str(p % 2), 16)

                preA = []
                if on("sync"):
                    preA = [(t + "x", 16 * (oI + NBLK)),
                            (t + "qt", 16 * (oI + NBLK))]
                if on("scalar"):
                    preA = [(t + "hqw0", 16 * (f * 4 + 4)),
                            (t + "hqw1", 16 * (f * 4 + 4))]
                if on("scalar"):
                    sems.wait(eng, t + "hqw0", 16 * (f * 4 + 4))
                    sems.wait(eng, t + "hqw1", 16 * (f * 4 + 4))
                    sems.wait(eng, "p_st2", 16)
                    d = eng.dma_start(out=W2b, in_=w2q_st[:, :])
                    sems.inc(d, t + "w2p", 16)
                join(t + str(f) + "A", preA)

                wstg2 = [AB[:, slotC_off + 2 * H + b * D:
                            slotC_off + 2 * H + (b + 1) * D]
                         for b in range(2)]
                at_srcs = [wk, wv, wr, wo]

                def at_wprep(n):
                    wi, jt = n // 8, n % 8
                    if on("gpsimd"):
                        sems.wait(eng, "at_wt" + str(n % 2), 16 * (n // 2))
                        d = eng.dma_start(
                            out=wstg2[n % 2][:],
                            in_=at_srcs[wi][128 * jt:128 * (jt + 1), :])
                        sems.inc(d, "at_wc", 16)
                    if on("sync"):
                        if n == 0:
                            sems.wait(eng, "p_st1", 16)
                            sems.wait(eng, t + "mm1", 256)
                        sems.wait(eng, "at_wc", 16 * (n + 1))
                        d = eng.dma_start_transpose(
                            slotA[:, wi * 8192:(wi + 1) * 8192]
                            .rearrange("p (c n) -> p c n", c=DC)
                            [:, :, 128 * jt:128 * (jt + 1)],
                            wstg2[n % 2][:])
                        sems.inc(d, "at_wt" + str(n % 2), 16)

                rb = [af(0 + b * D, D) for b in range(2)]
                ob = [af(2048 + b * D, D) for b in range(2)]
                for p in range(NPAIR):
                    gp = oP + p
                    if on("scalar"):
                        sems.wait(eng, t + "bm", 2 * gp - 2)
                        d = eng.dma_start(
                            out=hqB[p % 2][:],
                            in_=hq_dram[:, p * 2 * H:(p + 1) * 2 * H])
                        sems.inc(d, t + "bh" + str(p % 2), 16)
                    hqv = hqB[p % 2].rearrange("p (c n) -> p c n", c=HC)
                    for b in range(2):
                        j = 2 * p + b
                        gj = oI + j
                        if on("sync"):
                            sems.wait(eng, t + "bev", 2 * gj - 2)
                            d = eng.dma_start(
                                out=rb[j % 2][:],
                                in_=rin[128 * j:128 * (j + 1), :])
                            sems.inc(d, t + "br", 16)
                        if on("tensor"):
                            if j == 0:
                                sems.wait(eng, t + "w2p", 16 * (f + 1))
                            sems.wait(eng, t + "bh" + str(p % 2),
                                      16 * (f * 4 + p // 2 + 1))
                            sems.wait(eng, t + "bev", 2 * gj - 2)
                            pa = psum[2 * (j % 2)]
                            pb = psum[2 * (j % 2) + 1]
                            for k in range(HC):
                                lhsT = hqv[:, k, b * 128:(b + 1) * 128]
                                eng.matmul(pa[:], lhsT,
                                           W2b[:, k * D:k * D + 512],
                                           start=(k == 0),
                                           stop=(k == HC - 1))
                                mm = eng.matmul(
                                    pb[:], lhsT,
                                    W2b[:, k * D + 512:(k + 1) * D],
                                    start=(k == 0), stop=(k == HC - 1))
                            sems.inc(mm, t + "bm", 1)
                        if on("vector"):
                            sems.wait(eng, t + "bm", gj + 1)
                            sems.wait(eng, t + "br", 16 * (gj + 1))
                            sems.wait(eng, t + "bo" + str(j % 2),
                                      16 * (f * 8 + j // 2))
                            pa = psum[2 * (j % 2)]
                            pb = psum[2 * (j % 2) + 1]
                            eng.scalar_tensor_tensor(
                                ob[j % 2][:, 0:512], pa[:],
                                sc2st[:, j:j + 1],
                                rb[j % 2][:, 0:512], ALU.mult, ALU.add)
                            s2i = eng.scalar_tensor_tensor(
                                ob[j % 2][:, 512:1024], pb[:],
                                sc2st[:, j:j + 1],
                                rb[j % 2][:, 512:1024], ALU.mult, ALU.add)
                            sems.inc(s2i, t + "bev", 2)
                        if on("sync"):
                            sems.wait(eng, t + "bev", 2 * (gj + 1))
                            d = eng.dma_start(
                                out=rout[128 * j:128 * (j + 1), :],
                                in_=ob[j % 2][:])
                            sems.inc(d, t + "bo" + str(j % 2), 16)
                        if f == 0:
                            at_wprep(2 * j)
                            at_wprep(2 * j + 1)
                preB = []
                if on("sync"):
                    preB = [(t + "br", 16 * (oI + NBLK)),
                            (t + "bo0", 16 * (f * 8 + 8)),
                            (t + "bo1", 16 * (f * 8 + 8))]
                    if f == 0:
                        preB.append(("at_wt0", 16 * 16))
                        preB.append(("at_wt1", 16 * 16))
                if on("scalar"):
                    preB = [(t + "bh0", 16 * (f * 4 + 4)),
                            (t + "bh1", 16 * (f * 4 + 4))]
                if on("gpsimd") and f == 0:
                    preB = [("at_wc", 16 * 32)]
                join(t + str(f) + "B", preB)

            ffn(0)

            WK = slotA[:, 0:8192]
            WV = slotA[:, 8192:16384]
            WR = slotA[:, 16384:24576]
            WO = slotA[:, 24576:32768]
            kv_fm = af(0, 2048)
            scanb = [af(2048 + c * 257, 257) for c in range(DC)]
            xbA = [af(4112 + b * D, D) for b in range(2)]
            xnT = [AB[:, slotC_off + b * DC * TA:
                      slotC_off + (b + 1) * DC * TA] for b in range(2)]
            rw_bf = AB[:, slotC_off + 2 * DC * TA:slotC_off + 3 * DC * TA]
            r_bf = AB[:, slotC_off + 3 * DC * TA:slotC_off + 4 * DC * TA]
            xqb = AB[:, qs_off:qs_off + D]

            if on("scalar"):
                sems.wait(eng, "p_st1", 16)
                d = eng.dma_start(out=slotB, in_=w1q_st[:, :])
                sems.inc(d, "at_w1p", 16)

            if on("vector"):
                for c in range(DC):
                    eng.memset(scanb[c][:, 0:1], 0.0)

            for a in range(NA):
                for s_ in range(2):
                    i = 2 * a + s_
                    if on("sync"):
                        sems.wait(eng, "at_q", i - 1)
                        sems.wait(eng, "at_o", 16 * (i - 1))
                        d = eng.dma_start(out=xbA[i % 2][:],
                                          in_=resid1[128 * i:
                                                     128 * (i + 1), :])
                        sems.inc(d, "at_x", 16)
                    if on("vector"):
                        sems.wait(eng, "at_x", 16 * (i + 1))
                        op = ln_stats_v(xbA[i % 2])
                        sems.inc(op, "at_st", 1)
                        sems.wait(eng, "at_rs", i + 1)
                        eng.scalar_tensor_tensor(C["nmr"], C["m"], -1.0,
                                                 C["rstd"], ALU.mult,
                                                 ALU.mult)
                        sems.wait(eng, "at_qt", 16 * i)
                        eng.drain()
                        op = eng.tensor_scalar(xqb[:], xbA[i % 2][:],
                                               C["rstd"], C["nmr"], ALU.mult,
                                               ALU.add)
                        sems.inc(op, "at_q", 1)
                    if on("scalar"):
                        sems.wait(eng, "at_st", i + 1)
                        op = rstd_s()
                        sems.inc(op, "at_rs", 1)
                    if on("sync"):
                        sems.wait(eng, "at_q", i + 1)
                        sems.wait(eng, "at_rm", (a - 1) * DC)
                        d = eng.dma_start_transpose(
                            xnT[a % 2].rearrange("p (c n) -> p c n", c=DC)
                            [:, :, 128 * s_:128 * (s_ + 1)],
                            xqb[:])
                        sems.inc(d, "at_qt", 16)

                for kind, Wmat, pbase in [("k", WK, 0), ("v", WV, 2),
                                          ("r", WR, 4)]:
                    for ec in range(DC):
                        n = a * DC + ec
                        if on("tensor"):
                            if kind == "k" and ec == 0 and a == 0:
                                sems.wait(eng, "at_wt0", 16 * 16)
                                sems.wait(eng, "at_wt1", 16 * 16)
                            sems.wait(eng, "at_qt", 16 * (2 * a + 2))
                            sems.wait(eng, "at_" + kind + "e", n - 1)
                            ps = psum[pbase + ec % 2]
                            for k in range(DC):
                                mm = eng.matmul(
                                    ps[:, 0:TA],
                                    Wmat[:, k * D + ec * 128:
                                         k * D + (ec + 1) * 128],
                                    xnT[a % 2][:, k * TA:(k + 1) * TA],
                                    start=(k == 0), stop=(k == DC - 1))
                            sems.inc(mm, "at_" + kind + "m", 1)
                        if kind == "k" and on("scalar"):
                            sems.wait(eng, "at_km", n + 1)
                            sems.wait(eng, "at_rw", (a - 1) * DC + ec + 1)
                            op = eng.activation(
                                kv_fm[:, ec * TA:(ec + 1) * TA],
                                psum[0 + ec % 2][:, 0:TA],
                                AF.Identity, bias=0.0, scale=1.0)
                            sems.inc(op, "at_ke", 1)
                        if kind == "v" and on("vector"):
                            sems.wait(eng, "at_vm", n + 1)
                            sems.wait(eng, "at_ke", n + 1)
                            op = eng.tensor_mul(
                                kv_fm[:, ec * TA:(ec + 1) * TA],
                                psum[2 + ec % 2][:, 0:TA],
                                kv_fm[:, ec * TA:(ec + 1) * TA])
                            sems.inc(op, "at_ve", 1)
                        if kind == "r" and on("scalar"):
                            sems.wait(eng, "at_rm", n + 1)
                            sems.wait(eng, "at_rw", (a - 1) * DC + ec + 1)
                            op = eng.activation(
                                r_bf[:, ec * TA:(ec + 1) * TA],
                                psum[4 + ec % 2][:, 0:TA], AF.Sigmoid)
                            sems.inc(op, "at_re", 1)

                for c in range(DC):
                    n = a * DC + c
                    if on("vector"):
                        sems.wait(eng, "at_re", n + 1)
                        if c == 0:
                            sems.wait(eng, "at_om", 2 * (a - 1) + 2)
                        sl = kv_fm[:, c * TA:(c + 1) * TA]
                        eng.tensor_tensor_scan(
                            scanb[c][:, 1:TA + 1],
                            decay_c[:, c:c + 1].broadcast_to([128, TA]),
                            sl, scanb[c][:, 0:1], ALU.mult, ALU.add)
                        eng.scalar_tensor_tensor(
                            sl, sl, eu_c[:, c:c + 1], scanb[c][:, 0:TA],
                            ALU.mult, ALU.add)
                        eng.tensor_copy(scanb[c][:, 0:1],
                                        scanb[c][:, TA:TA + 1])
                        op = eng.tensor_mul(rw_bf[:, c * TA:(c + 1) * TA],
                                            r_bf[:, c * TA:(c + 1) * TA], sl)
                        sems.inc(op, "at_rw", 1)

                for s_ in range(2):
                    i = 2 * a + s_
                    if on("tensor"):
                        sems.wait(eng, "at_rw", (a + 1) * DC)
                        sems.wait(eng, "at_oe", i)
                        for k in range(DC):
                            lhsT = rw_bf[:, k * TA + s_ * 128:
                                         k * TA + s_ * 128 + 128]
                            eng.matmul(psum[6][:], lhsT,
                                       WO[:, k * D:k * D + 512],
                                       start=(k == 0), stop=(k == DC - 1))
                            mm = eng.matmul(psum[7][:], lhsT,
                                            WO[:, k * D + 512:(k + 1) * D],
                                            start=(k == 0),
                                            stop=(k == DC - 1))
                        sems.inc(mm, "at_om", 1)
                    if on("vector"):
                        sems.wait(eng, "at_om", i + 1)
                        xsl = xbA[i % 2]
                        eng.tensor_add(xsl[:, 0:512], psum[6][:],
                                       xsl[:, 0:512])
                        ev = eng.tensor_add(xsl[:, 512:1024], psum[7][:],
                                            xsl[:, 512:1024])
                        sems.inc(ev, "at_oe", 1)
                    if on("sync"):
                        sems.wait(eng, "at_oe", i + 1)
                        d = eng.dma_start(out=resid2[128 * i:128 * (i + 1), :],
                                          in_=xbA[i % 2][:])
                        sems.inc(d, "at_o", 16)

            pre_at = []
            if on("sync"):
                pre_at = [("at_x", 16 * NBLK), ("at_qt", 16 * NBLK),
                          ("at_o", 16 * NBLK)]
            if on("scalar"):
                pre_at = [("at_w1p", 16)]
            join("attn", pre_at)
            ffn(1)
            if on("sync"):
                sems.wait(eng, "Fbo0", 16 * 16)
                sems.wait(eng, "Fbo1", 16 * 16)

        @block.sync
        def _(sync):
            emit("sync", sync)

        @block.scalar
        def _(scalar):
            emit("scalar", scalar)

        @block.vector
        def _(vector):
            emit("vector", vector)

        @block.gpsimd
        def _(gpsimd):
            emit("gpsimd", gpsimd)

        @block.tensor
        def _(tensor):
            emit("tensor", tensor)

        sems.check()

    mybir.codegen_inst_isa_subclasses(nc)
    return nc


_NC_CACHE = {}


def kernel(L0_attn_x, L0_attn_Wk, L0_attn_Wv, L0_attn_Wr, L0_attn_Wo,
           L0_attn_time_decay, L0_attn_time_first, L0_ffn_w1, L0_ffn_conv_w,
           L0_ffn_conv_b, L0_ffn_w2):
    op_rmc, op_qs, op_rm = make_ops()
    if "nc" not in _NC_CACHE:
        _NC_CACHE["nc"] = build_nc(op_rmc, op_qs, op_rm)
    nc = _NC_CACHE["nc"]
    x = np.ascontiguousarray(np.asarray(L0_attn_x, dtype=np.float32))
    shared = {
        "wk": np.ascontiguousarray(np.asarray(L0_attn_Wk, np.float32)),
        "wv": np.ascontiguousarray(np.asarray(L0_attn_Wv, np.float32)),
        "wr": np.ascontiguousarray(np.asarray(L0_attn_Wr, np.float32)),
        "wo": np.ascontiguousarray(np.asarray(L0_attn_Wo, np.float32)),
        "td": np.ascontiguousarray(np.asarray(L0_attn_time_decay,
                                              np.float32)),
        "tf": np.ascontiguousarray(np.asarray(L0_attn_time_first,
                                              np.float32)),
        "w1": np.ascontiguousarray(np.asarray(L0_ffn_w1, np.float32)),
        "cw": np.ascontiguousarray(np.asarray(L0_ffn_conv_w, np.float32)),
        "cb": np.ascontiguousarray(np.asarray(L0_ffn_conv_b, np.float32)),
        "w2": np.ascontiguousarray(np.asarray(L0_ffn_w2, np.float32)),
    }
    in_maps = [dict(shared, x=np.ascontiguousarray(x[b])) for b in range(8)]
    res = run_bass_kernel_spmd(nc, in_maps, core_ids=list(range(8)))
    return np.stack([r["out"] for r in res.results], axis=0)


if __name__ == "__main__":
    nc = build_nc(*make_ops())
    n_inst = sum(1 for _ in nc.all_instructions())
    print("built ok,", n_inst, "instructions")


# version 12
# speedup vs baseline: 1.0685x; 1.0685x over previous
import contextlib
import numpy as np
import concourse.bass as bass
import concourse.mybir as mybir
from concourse.bass_utils import run_bass_kernel_spmd
import concourse.dve_ops as dve_ops
from concourse.dve_op import DveOp
from concourse.dve_spec import Spec, Src0, Src1, C0, C1, C2, maxx, minn
from concourse.dve_ops_raw import _compute_dtype

F32 = mybir.dt.float32
BF16 = mybir.dt.bfloat16
AF = mybir.ActivationFunctionType
ALU = mybir.AluOpType
AX = mybir.AxisListType
MAGIC = 12582912.0

D = 1024
S = 2048
H = 4096
DC = 8
HC = 32
TB = 128
NBLK = S // TB
NPAIR = NBLK // 2
TA = 256
NA = S // TA
QW = 2080
QOFF = 16
EPS = 1e-5


def _ref_rmc(in0, in1, s0, s1, imm2):
    f = _compute_dtype(in0)
    return np.clip(np.round(in0.astype(f) * in1.astype(f)), s0, s1).astype(f)


def _ref_qs(in0, in1, s0, s1, imm2):
    f = _compute_dtype(in0)
    return (np.round(in0.astype(f) * s0) * s1).astype(f)


def _sha_of(body, ref):
    probe = DveOp("TMP_SHA", Spec(body=body), subdim=False, uops_sha={},
                  reference=ref)
    try:
        probe.compile("v3")
    except ValueError as exc:
        msg = str(exc)
        key = 'uops_sha["v3"]="'
        i = msg.index(key) + len(key)
        return msg[i:msg.index('"', i)]
    raise RuntimeError("expected sha mismatch error")


def _reg(name, body, ref):
    for op in dve_ops.OPS:
        if op.name == name:
            return op
    op = DveOp(name, Spec(body=body), subdim=False,
               uops_sha={"v3": _sha_of(body, ref)}, reference=ref)
    dve_ops.OPS.append(op)
    dve_ops.CUSTOM_DVE_SPECS[op.name] = op.spec
    dve_ops.CUSTOM_DVE_REFS[op.name] = op.reference
    row = max(dve_ops._SUB_OPCODE_FOR_NAME.values()) + 1
    assert row < 0x20, "no custom-DVE opcode row left"
    dve_ops._SUB_OPCODE_FOR_NAME[op.name] = row
    return op


def _ref_rm(in0, in1, s0, s1, imm2):
    f = _compute_dtype(in0)
    return np.round(in0.astype(f) * in1.astype(f)).astype(f)


def _ref_c2(in0, in1, s0, s1, imm2):
    f = _compute_dtype(in0)
    return (in0.astype(f) * s0 + in1.astype(f) * s1).astype(f)


def make_ops():
    op_rm = _reg("ROUND_MUL", (Src0 * Src1 + C0) - C0, _ref_rm)
    op_qs = _reg("QUANT_SCALE", ((Src0 * C0 + C2) - C2) * C1, _ref_qs)
    return op_rm, op_qs


class Sems:

    def __init__(self, nc, es):
        self.nc = nc
        self.es = es
        self.h = {}
        self.incs = {}
        self.maxw = {}

    def _get(self, name):
        if name not in self.h:
            self.h[name] = self.es.enter_context(self.nc.semaphore(name))
            self.incs[name] = 0
            self.maxw[name] = 0
        return self.h[name]

    def inc(self, instr, name, amount=1):
        instr.then_inc(self._get(name), amount)
        self.incs[name] += amount

    def wait(self, eng, name, value):
        if value <= 0:
            self._get(name)
            return
        sem = self._get(name)
        self.maxw[name] = max(self.maxw[name], value)
        eng.wait_ge(sem, value)

    def check(self):
        bad = [(k, self.maxw[k], self.incs[k]) for k in self.h
               if self.maxw[k] > self.incs[k]]
        assert not bad, f"unsatisfiable semaphore waits: {bad}"


def build_nc(op_rm, op_qs, dbg=False):
    nc = bass.Bass("TRN2", target_bir_lowering=False,
                   detect_race_conditions=False)
    x_in = nc.dram_tensor("x", [S, D], F32, kind="ExternalInput").ap()
    wk = nc.dram_tensor("wk", [D, D], F32, kind="ExternalInput").ap()
    wv = nc.dram_tensor("wv", [D, D], F32, kind="ExternalInput").ap()
    wr = nc.dram_tensor("wr", [D, D], F32, kind="ExternalInput").ap()
    wo = nc.dram_tensor("wo", [D, D], F32, kind="ExternalInput").ap()
    td = nc.dram_tensor("td", [D], F32, kind="ExternalInput").ap()
    tf = nc.dram_tensor("tf", [D], F32, kind="ExternalInput").ap()
    w1 = nc.dram_tensor("w1", [H, D], F32, kind="ExternalInput").ap()
    cw = nc.dram_tensor("cw", [H, 1, 3], F32, kind="ExternalInput").ap()
    cb = nc.dram_tensor("cb", [H], F32, kind="ExternalInput").ap()
    w2 = nc.dram_tensor("w2", [D, H], F32, kind="ExternalInput").ap()
    out = nc.dram_tensor("out", [S, D], F32, kind="ExternalOutput").ap()
    _ik = "ExternalOutput" if dbg else "Internal"
    resid1 = nc.dram_tensor("resid1", [S, D], F32, kind=_ik).ap()
    resid2 = nc.dram_tensor("resid2", [S, D], F32, kind=_ik).ap()
    w1q_st = nc.dram_tensor("w1q_st", [128, DC * H], BF16, kind=_ik).ap()
    w2q_st = nc.dram_tensor("w2q_st", [128, HC * D], BF16, kind=_ik).ap()
    hq_dram = nc.dram_tensor("hq_dram", [128, NBLK * H], BF16,
                             kind=_ik).ap()

    with contextlib.ExitStack() as es:
        e = es.enter_context
        ident = e(nc.sbuf_tensor("ident", [128, 128], F32))
        ones128 = e(nc.sbuf_tensor("ones128", [128, 128], F32))
        cw_c = e(nc.sbuf_tensor("cw_c", [128, 96], F32))
        cwm = e(nc.sbuf_tensor("cwm", [128, 96], F32))
        cb_c = e(nc.sbuf_tensor("cb_c", [128, 32], F32))
        td_c = e(nc.sbuf_tensor("td_c", [128, 8], F32))
        tf_c = e(nc.sbuf_tensor("tf_c", [128, 8], F32))
        decay_c = e(nc.sbuf_tensor("decay_c", [128, 8], F32))
        eu_c = e(nc.sbuf_tensor("eu_c", [128, 8], F32))
        ccol = e(nc.sbuf_tensor("ccol", [128, 36], F32))
        sc2st = e(nc.sbuf_tensor("sc2st", [128, NBLK], F32))
        s2row = e(nc.sbuf_tensor("s2row", [1, 128], F32))
        bn12 = e(nc.sbuf_tensor("bn12", [128, 12], F32))
        msq4 = e(nc.sbuf_tensor("msq4", [128, 4], F32))
        jnk = e(nc.sbuf_tensor("jnk", [128, 4], F32))
        jnkb = e(nc.sbuf_tensor("jnkb", [128, 128], BF16))
        AB = e(nc.sbuf_tensor("AB", [128, 91392], BF16,
                              is_transpose_dest=True))
        AF32 = e(nc.sbuf_tensor("AF", [128, 6160], F32))
        slotA = AB[:, 0:32768]
        slotB = AB[:, 32768:65536]
        slotC_off = 65536
        silu_off = 82176
        qs_off = 90368
        psum = [e(nc.psum_tensor(f"ps{i}", [128, 512], F32)) for i in range(8)]
        epsc = ccol[:, 30:31]
        nhc = ccol[:, 31:32]
        nc.const_aps.register(F32, EPS, epsc)
        nc.const_aps.register(F32, -0.5, nhc)
        nc.const_aps.register(F32, MAGIC, ccol[:, 34:35])
        nc.const_aps.register(F32, -MAGIC, ccol[:, 35:36])
        sems = Sems(nc, es)
        block = e(nc.Block())

        colnames = ["acc", "mean1c", "m1inv", "mean2c", "m2inv", "m2h",
                    "m", "var", "rstd", "nmr", "mx", "mxc", "rc", "s1",
                    "inv1", "mxh", "i2", "s2", "tmp", "tmp2", "e1a",
                    "stmp"]
        CCI = {k: i for i, k in enumerate(colnames)}
        CCI["acc2"] = 32
        CCI["tmpb"] = 33

        def emit(role, eng):
            def on(r):
                return role == r

            C = {k: ccol[:, i:i + 1] for k, i in CCI.items()}
            e1cols = ccol[:, 22:30]

            def join(tag, pre=()):
                nm = "join_" + tag
                for snm, val in pre:
                    sems.wait(eng, snm, val)
                if on("vector"):
                    op = eng.memset(jnk[:, 0:1], 0.0)
                elif on("scalar"):
                    op = eng.activation(jnk[:, 1:2], ccol[:, 30:31], AF.Copy,
                                        bias=0.0, scale=1.0)
                elif on("gpsimd"):
                    op = eng.memset(jnk[:, 2:3], 0.0)
                elif on("tensor"):
                    op = eng.ldweights(jnkb[:, 0:128])
                else:
                    op = eng.nop()
                sems.inc(op, nm, 1)
                sems.wait(eng, nm, 5)

            def af(off, n):
                return AF32[:, off:off + n]

            if on("vector"):
                eng.memset(ident[:], 0.0)
                eng.memset(ones128[:], 1.0)
                eng.memset(jnk[:], 0.0)
                eng.memset(jnkb[:], 0.0)
                eng.memset(ccol[:, 30:31], EPS)
                eng.memset(ccol[:, 34:35], MAGIC)
                eng.memset(ccol[:, 35:36], -MAGIC)
                op = eng.memset(ccol[:, 31:32], -0.5)
                sems.inc(op, "v_memset", 1)
            if on("gpsimd"):
                sems.wait(eng, "v_memset", 1)
                op = eng.affine_select(out=ident[:], in_=ident[:],
                                        compare_op=ALU.not_equal, fill=1.0,
                                        base=0, pattern=[[-1, 128]],
                                        channel_multiplier=1)
                sems.inc(op, "g_const", 1)

            if on("sync"):
                d1 = eng.dma_start(
                    out=cw_c[:, 0:96].rearrange("p (c k) -> p c k", c=32),
                    in_=cw.rearrange("(c p) one k -> p c (one k)", p=128))
                sems.inc(d1, "p_small", 16)
                d2 = eng.dma_start(out=cb_c[:, 0:32],
                                   in_=cb.rearrange("(c p) -> p c", p=128),
                                   allow_slow_non_contiguous=True)
                sems.inc(d2, "p_small", 16)
                d3 = eng.dma_start(out=td_c[:, 0:8],
                                   in_=td.rearrange("(c p) -> p c", p=128),
                                   allow_slow_non_contiguous=True)
                sems.inc(d3, "p_small", 16)
                d4 = eng.dma_start(out=tf_c[:, 0:8],
                                   in_=tf.rearrange("(c p) -> p c", p=128),
                                   allow_slow_non_contiguous=True)
                sems.inc(d4, "p_small", 16)

            if on("scalar"):
                sems.wait(eng, "p_small", 64)
                op = eng.activation(e1cols, td_c[:], AF.Exp)
                sems.inc(op, "p_exp1", 1)
                sems.wait(eng, "p_neg", 1)
                eng.activation(decay_c[:], e1cols, AF.Exp)
                op = eng.activation(eu_c[:], tf_c[:], AF.Exp)
                sems.inc(op, "p_exp2", 1)
            if on("vector"):
                sems.wait(eng, "p_exp1", 1)
                op = eng.tensor_scalar(e1cols, e1cols, -1.0, None, ALU.mult)
                sems.inc(op, "p_neg", 1)

            w1stg = [af(0 + b * D, D) for b in range(2)]
            w2stg = [af(2048 + b * 2048, 2048) for b in range(2)]
            w1rt = [AB[:, slotC_off + b * D:slotC_off + (b + 1) * D]
                    for b in range(2)]
            w2rt = [AB[:, slotC_off + 2048 + b * 2048:
                       slotC_off + 2048 + (b + 1) * 2048] for b in range(2)]

            def w2src(j):
                return w2[128 * (j // 2):128 * (j // 2) + 128,
                          (j % 2) * 2048:(j % 2 + 1) * 2048]

            if on("gpsimd"):
                for i in range(32):
                    sems.wait(eng, "pw1s", i - 1)
                    d = eng.dma_start(out=w1stg[i % 2][:],
                                      in_=w1[128 * i:128 * (i + 1), :])
                    sems.inc(d, "pw1a" + str(i % 2), 16)
                for i in range(32):
                    sems.wait(eng, "pw1s", (31 + i) if i < 2 else 0)
                    sems.wait(eng, "pw1r", i - 1)
                    d = eng.dma_start(out=w1stg[i % 2][:],
                                      in_=w1[128 * i:128 * (i + 1), :])
                    sems.inc(d, "pw1a" + str(i % 2), 16)
            def w2_tr(eng, j):
                sems.wait(eng, "pw2r", j + 1)
                d = eng.dma_start_transpose(
                    slotB.rearrange("p (c n) -> p c n", c=HC)
                    [:, 16 * (j % 2):16 * (j % 2) + 16,
                     128 * (j // 2):128 * (j // 2) + 128],
                    w2rt[j % 2][:])
                sems.inc(d, "pw2t" + str(j % 2), 16)

            if on("scalar"):
                for j in range(16):
                    sems.wait(eng, "pw2s", j - 1)
                    d = eng.dma_start(out=w2stg[j % 2][:], in_=w2src(j))
                    sems.inc(d, "pw2a" + str(j % 2), 16)
                for j in range(16):
                    sems.wait(eng, "pw2s", (15 + j) if j < 2 else 0)
                    sems.wait(eng, "pw2r", j - 1)
                    d = eng.dma_start(out=w2stg[j % 2][:], in_=w2src(j))
                    sems.inc(d, "pw2a" + str(j % 2), 16)
            if on("tensor"):
                sems.wait(eng, "g_const", 1)
                sems.wait(eng, "pw1s", 32)
                mm = eng.matmul(psum[7][:, 0:1], ones128[:], C["acc"],
                                start=True, stop=True)
                sems.inc(mm, "pw1pe", 1)
                sems.wait(eng, "pw2s", 16)
                mm = eng.matmul(psum[7][:, 1:2], ones128[:], C["acc2"],
                                start=True, stop=True)
                sems.inc(mm, "pw2pe", 1)
            if on("vector"):
                eng.memset(C["acc"], 0.0)
                eng.memset(C["acc2"], 0.0)
                for i in range(32):
                    sems.wait(eng, "pw1a" + str(i % 2), 16 * (i // 2 + 1))
                    eng.tensor_reduce(C["tmp"], w1stg[i % 2][:], axis=AX.X,
                                      op=ALU.add, apply_absolute_value=True)
                    eng.drain()
                    op = eng.tensor_add(C["acc"], C["acc"], C["tmp"])
                    sems.inc(op, "pw1s", 1)
                    if i % 2 == 1:
                        j = i // 2
                        sems.wait(eng, "pw2a" + str(j % 2),
                                  16 * (j // 2 + 1))
                        eng.tensor_reduce(C["tmpb"], w2stg[j % 2][:],
                                          axis=AX.X, op=ALU.add,
                                          apply_absolute_value=True)
                        eng.drain()
                        op = eng.tensor_add(C["acc2"], C["acc2"], C["tmpb"])
                        sems.inc(op, "pw2s", 1)
                sems.wait(eng, "pw1pe", 1)
                eng.tensor_scalar(C["mean1c"], psum[7][:, 0:1],
                                  1.0 / (32 * 128 * D), 1e-5,
                                  ALU.mult, ALU.max)
                eng.drain()
                eng.reciprocal(C["m1inv"], C["mean1c"])
                eng.drain()
                sems.wait(eng, "pw2pe", 1)
                eng.tensor_scalar(C["mean2c"], psum[7][:, 1:2],
                                  1.0 / (16 * 128 * 2048), 1e-5,
                                  ALU.mult, ALU.max)
                eng.drain()
                eng.reciprocal(C["m2inv"], C["mean2c"])
                eng.drain()
                for i in range(32):
                    sems.wait(eng, "pw1a" + str(i % 2),
                              16 * (16 + i // 2 + 1))
                    sems.wait(eng, "pw1t" + str(i % 2), 16 * (i // 2))
                    eng._custom_dve(
                        op_rm, out=w1rt[i % 2][:], in0=w1stg[i % 2][:],
                        in1=ccol[:, CCI["m1inv"]:CCI["m1inv"] + 1]
                        .broadcast_to([128, D]),
                        s0=MAGIC, s1=0.0, imm2=0.0)
                    eng.drain()
                    op = eng.tensor_scalar(w1rt[i % 2][:], w1rt[i % 2][:],
                                           1.0, -1.0, ALU.min, ALU.max)
                    sems.inc(op, "pw1r", 1)
                    if i % 2 == 1:
                        j = i // 2
                        sems.wait(eng, "pw2a" + str(j % 2),
                                  16 * (8 + j // 2 + 1))
                        sems.wait(eng, "pw2t" + str(j % 2), 16 * (j // 2))
                        eng._custom_dve(
                            op_rm, out=w2rt[j % 2][:], in0=w2stg[j % 2][:],
                            in1=ccol[:, CCI["m2inv"]:CCI["m2inv"] + 1]
                            .broadcast_to([128, 2048]),
                            s0=MAGIC, s1=0.0, imm2=0.0)
                        eng.drain()
                        op = eng.tensor_scalar(w2rt[j % 2][:],
                                               w2rt[j % 2][:],
                                               1.0, -1.0, ALU.min, ALU.max)
                        sems.inc(op, "pw2r", 1)
            if on("sync"):
                for i in range(32):
                    sems.wait(eng, "pw1r", i + 1)
                    d = eng.dma_start_transpose(
                        slotA.rearrange("p (c n) -> p c n", c=DC)
                        [:, :, 128 * i:128 * (i + 1)],
                        w1rt[i % 2][:])
                    sems.inc(d, "pw1t" + str(i % 2), 16)
                    if i % 2 == 1:
                        w2_tr(eng, i // 2)
            if on("vector"):
                sems.wait(eng, "p_small", 64)
                eng.tensor_scalar(cwm[:], cw_c[:], C["mean1c"], None,
                                  ALU.mult)
                eng.tensor_scalar(C["m2h"], C["mean2c"], 0.5, None,
                                  ALU.mult)
            if on("scalar"):
                sems.wait(eng, "pw1t0", 16 * 16)
                sems.wait(eng, "pw1t1", 16 * 16)
                d = eng.dma_start(out=w1q_st[:, :], in_=slotA)
                sems.inc(d, "p_st1", 16)
                sems.wait(eng, "pw2t0", 16 * 8)
                sems.wait(eng, "pw2t1", 16 * 8)
                d = eng.dma_start(out=w2q_st[:, :], in_=slotB)
                sems.inc(d, "p_st2", 16)

            prep_pre = []
            if on("sync"):
                prep_pre = [("pw1t0", 16 * 16), ("pw1t1", 16 * 16),
                            ("p_small", 64)]
            if on("scalar"):
                prep_pre = [("pw2a0", 16 * 16), ("pw2a1", 16 * 16),
                            ("pw2t0", 16 * 8), ("pw2t1", 16 * 8)]
            if on("gpsimd"):
                prep_pre = [("pw1a0", 16 * 32), ("pw1a1", 16 * 32)]
            join("prep", prep_pre)

            def ln_stats_v(x_):
                eng.bn_stats(bn12[:, 0:6], x_[:, 0:512])
                eng.bn_stats(bn12[:, 6:12], x_[:, 512:1024])
                eng.drain()
                m3 = bn12.rearrange("p (g three) -> p g three", three=3)
                t3 = bn12.rearrange("p (g three) -> p three g", three=3)
                eng.tensor_reduce(C["m"], t3[:, 1:2, :], axis=AX.X,
                                  op=ALU.add)
                eng.tensor_reduce(C["var"], t3[:, 2:3, :], axis=AX.X,
                                  op=ALU.add)
                eng.tensor_mul(msq4[:], m3[:, :, 1:2], m3[:, :, 1:2])
                eng.drain()
                eng.tensor_reduce(C["tmp"], msq4[:], axis=AX.X, op=ALU.add)
                eng.tensor_scalar(C["m"], C["m"], 0.25, None, ALU.mult)
                eng.drain()
                eng.tensor_scalar(C["tmp"], C["tmp"], 0.25, None, ALU.mult)
                eng.tensor_mul(C["tmp2"], C["m"], C["m"])
                eng.drain()
                eng.scalar_tensor_tensor(C["var"], C["var"], 1.0 / 1024.0,
                                         C["tmp"], ALU.mult, ALU.add)
                eng.drain()
                return eng.tensor_sub(C["var"], C["var"], C["tmp2"])

            def rstd_s():
                eng.activation(C["stmp"], C["var"], AF.Ln, bias=EPS,
                               scale=1.0)
                return eng.activation(C["rstd"], C["stmp"], AF.Exp, bias=0.0,
                                      scale=-0.5)

            def ffn(f):
                t = "F"
                oI = f * NBLK
                oP = f * NPAIR
                oC = f * NPAIR * HC
                rin = x_in if f == 0 else resid2
                rout = resid1 if f == 0 else out
                W1b = slotA if f == 0 else slotB
                W2b = slotB if f == 0 else slotA
                qT = AB[:, slotC_off:slotC_off + DC * QW]
                qTv = qT.rearrange("p (c n) -> p c n", c=DC)
                silu2 = [AB[:, silu_off:silu_off + 2 * H], W2b[:, 0:2 * H]]
                ysc = [af(128 * 2 * m, 256) for m in range(2)]
                xbv = [af(4096, D), af(2048, D)]
                xnv = [af(5120, D), af(3072, D)]
                Mb = af(512, 128)
                tmpc2 = [af(1536 + 256 * m, 256) for m in range(2)]
                qsb = AB[:, qs_off:qs_off + D]
                hqB = [AB[:, slotC_off:slotC_off + 2 * H],
                       AB[:, silu_off:silu_off + 2 * H]]

                if on("vector"):
                    eng.memset(qTv[:, :, QOFF - 1:QOFF], 0.0)
                    op = eng.memset(qTv[:, :, QOFF + S:QOFF + S + 1], 0.0)
                    sems.inc(op, t + "hz", 1)

                def xside_v(i):
                    gi = oI + i
                    xb, xn = xbv[i % 2], xnv[i % 2]
                    sems.wait(eng, t + "x", 16 * (gi + 1))
                    op = ln_stats_v(xb)
                    sems.inc(op, t + "st", 1)
                    sems.wait(eng, t + "rs", gi + 1)
                    eng.scalar_tensor_tensor(C["nmr"], C["m"], -1.0,
                                             C["rstd"], ALU.mult, ALU.mult)
                    sems.wait(eng, t + "q", gi - 1)
                    eng.drain()
                    eng.tensor_scalar(xn[:], xb[:], C["rstd"],
                                      C["nmr"], ALU.mult, ALU.add)
                    eng.reduce_max(C["mx"], xn[:], axis=AX.X,
                                   apply_absolute_value=True)
                    eng.drain()
                    eng.tensor_scalar(C["inv1"], C["mx"], 1e-5, 1.0 / 127.0,
                                      ALU.max, ALU.mult)
                    eng.drain()
                    eng.reciprocal(C["s1"], C["inv1"])
                    op = eng.engine_nop()
                    sems.inc(op, t + "xq", 1)
                    sems.wait(eng, t + "qt", 16 * gi)
                    eng.drain()
                    op = eng._custom_dve(op_qs, out=qsb[:], in0=xn[:],
                                         s0=C["s1"], s1=C["inv1"],
                                         imm2=MAGIC)
                    sems.inc(op, t + "q", 1)

                def xside_s(i):
                    gi = oI + i
                    sems.wait(eng, t + "st", gi + 1)
                    op = rstd_s()
                    sems.inc(op, t + "rs", 1)

                if on("sync"):
                    for i in range(NBLK):
                        gi = oI + i
                        sems.wait(eng, t + "xq", gi - 1)
                        d = eng.dma_start(out=xbv[i % 2][:],
                                          in_=rin[128 * i:128 * (i + 1), :])
                        sems.inc(d, t + "x", 16)
                        sems.wait(eng, t + "q", gi + 1)
                        d = eng.dma_start_transpose(
                            qTv[:, :, QOFF + 128 * i:QOFF + 128 * (i + 1)],
                            qsb[:])
                        sems.inc(d, t + "qt", 16)

                for p in range(NPAIR):
                    tiles = list(range(4)) if p == 0 else (
                        [2 * p + 2, 2 * p + 3] if 2 * p + 3 < NBLK else [])
                    for i in tiles:
                        if on("vector"):
                            xside_v(i)
                        if on("scalar"):
                            xside_s(i)
                    silu = silu2[p % 2]
                    siluv = silu.rearrange("p (c n) -> p c n", c=HC)
                    for c in range(HC):
                        npc = oC + p * HC + c
                        if on("tensor"):
                            if p == 0 and c == 0:
                                sems.wait(eng, t + "hz", f + 1)
                                if f == 1:
                                    sems.wait(eng, "at_w1p", 16)
                            sems.wait(eng, t + "qt",
                                      16 * (oI + min(2 * p + 3, NBLK)))
                            sems.wait(eng, t + "p1", npc - 3)
                            sems.wait(eng, t + "cv", npc - 3)
                            pa = psum[c % 4]
                            for k in range(DC):
                                lhsT = W1b[:, k * H + c * 128:
                                           k * H + (c + 1) * 128]
                                base = k * QW + QOFF + 256 * p
                                mm = eng.matmul(pa[:, 0:258], lhsT,
                                                qT[:, base - 1:base + 257],
                                                start=(k == 0),
                                                stop=(k == DC - 1))
                            sems.inc(mm, t + "mm1", 1)
                        ps = psum[c % 4]
                        ssl = silu[:, c * 256:(c + 1) * 256]
                        yv = ysc[c % 2]
                        if on("scalar"):
                            if c == 0:
                                if p == 1 and f == 0:
                                    sems.wait(eng, "p_st2", 16)
                                sems.wait(eng, t + "hqw" + str(p % 2),
                                          16 * (f * 4 + p // 2))
                            sems.wait(eng, t + "cv", npc - 1)
                            sems.wait(eng, t + "mm1", npc + 1)
                            op = eng.activation(
                                yv[:], ps[:, 1:257], AF.Identity,
                                bias=cb_c[:, c:c + 1],
                                scale=cwm[:, 3 * c + 1:3 * c + 2])
                            sems.inc(op, t + "p1", 1)
                        if on("vector"):
                            sems.wait(eng, t + "p1", npc + 1)
                            eng.scalar_tensor_tensor(
                                yv[:], ps[:, 0:256],
                                cwm[:, 3 * c:3 * c + 1], yv[:],
                                ALU.mult, ALU.add)
                            eng.drain()
                            s2i = eng.scalar_tensor_tensor(
                                ssl, ps[:, 2:258],
                                cwm[:, 3 * c + 2:3 * c + 3], yv[:],
                                ALU.mult, ALU.add)
                            sems.inc(s2i, t + "cv", 1)
                    gp = oP + p
                    if on("scalar"):
                        sems.wait(eng, t + "cv", oC + p * HC + HC // 2)
                        op = eng.activation(silu[:, 0:H], silu[:, 0:H],
                                            AF.Silu)
                        sems.inc(op, t + "sl", 1)
                        sems.wait(eng, t + "cv", oC + (p + 1) * HC)
                        op = eng.activation(silu[:, H:2 * H],
                                            silu[:, H:2 * H], AF.Silu)
                        sems.inc(op, t + "sl", 1)
                    for b in range(2):
                        j = 2 * p + b
                        gj = oI + j
                        bview = siluv[:, :, b * 128:(b + 1) * 128]
                        if on("vector"):
                            sems.wait(eng, t + "sl", 2 * gp + 2)
                            op = eng.reduce_max(
                                Mb.rearrange("p (n one) -> p n one", one=1),
                                bview.rearrange("p c n -> p n c"),
                                axis=AX.X, apply_absolute_value=True)
                            sems.inc(op, t + "mx1", 1)
                        if on("tensor"):
                            sems.wait(eng, t + "mx1", gj + 1)
                            sems.wait(eng, t + "mx2", gj)
                            tr = eng.transpose(psum[4][:, 0:128], Mb[:],
                                               ident[:])
                            sems.inc(tr, t + "mxT", 1)
                        if on("vector"):
                            sems.wait(eng, t + "mxT", gj + 1)
                            eng.reduce_max(C["mxh"], psum[4][:, 0:128],
                                           axis=AX.X)
                            eng.drain()
                            eng.tensor_scalar(C["i2"], C["mxh"], 1e-5,
                                              1.0 / 127.0, ALU.max, ALU.mult)
                            eng.drain()
                            eng.reciprocal(C["s2"], C["i2"])
                            op = eng.tensor_mul(sc2st[:, j:j + 1], C["i2"],
                                                C["m2h"])
                            sems.inc(op, t + "mx2", 1)
                        if on("tensor"):
                            sems.wait(eng, t + "mx2", gj + 1)
                            sems.wait(eng, t + "s2r", gj)
                            tr = eng.transpose(psum[5][0:1, 0:128], C["s2"],
                                               ident[:])
                            sems.inc(tr, t + "s2T", 1)
                        if on("vector"):
                            sems.wait(eng, t + "s2T", gj + 1)
                            op = eng.tensor_copy(s2row[:],
                                                 psum[5][0:1, 0:128])
                            sems.inc(op, t + "s2r", 1)
                        if on("tensor"):
                            sems.wait(eng, t + "s2r", gj + 1)
                            sems.wait(eng, t + "rq", gp)
                            mm = eng.matmul(
                                psum[6][:, b * 128:(b + 1) * 128],
                                ones128[0:1, 0:128],
                                s2row[:], start=True, stop=True)
                            sems.inc(mm, t + "s2b", 1)
                    if on("vector"):
                        sems.wait(eng, t + "s2b", 2 * gp + 2)
                        op = eng._custom_dve(
                            op_rm,
                            out=siluv[:, :, :],
                            in0=siluv[:, :, :],
                            in1=psum[6][:, 0:256]
                            .rearrange("p (c n) -> p c n", c=1)
                            .broadcast_to([128, HC, 256]),
                            s0=MAGIC, s1=0.0, imm2=0.0)
                        sems.inc(op, t + "rq", 1)
                    if on("gpsimd"):
                        sems.wait(eng, t + "rq", gp + 1)
                        d = eng.dma_start(
                            out=hq_dram[:, p * 2 * H:(p + 1) * 2 * H],
                            in_=silu[:])
                        sems.inc(d, t + "hqw" + str(p % 2), 16)

                preA = []
                if on("sync"):
                    preA = [(t + "x", 16 * (oI + NBLK)),
                            (t + "qt", 16 * (oI + NBLK))]
                if on("gpsimd"):
                    preA = [(t + "hqw0", 16 * (f * 4 + 4)),
                            (t + "hqw1", 16 * (f * 4 + 4))]
                if on("scalar"):
                    sems.wait(eng, t + "hqw0", 16 * (f * 4 + 4))
                    sems.wait(eng, t + "hqw1", 16 * (f * 4 + 4))
                    sems.wait(eng, "p_st2", 16)
                    d = eng.dma_start(out=W2b, in_=w2q_st[:, :])
                    sems.inc(d, t + "w2p", 16)
                join(t + str(f) + "A", preA)

                wstg2 = [AB[:, slotC_off + 2 * H + b * D:
                            slotC_off + 2 * H + (b + 1) * D]
                         for b in range(2)]
                at_srcs = [wk, wv, wr, wo]

                def at_wprep(n):
                    wi, jt = n // 8, n % 8
                    if on("gpsimd"):
                        sems.wait(eng, "at_wt" + str(n % 2), 16 * (n // 2))
                        d = eng.dma_start(
                            out=wstg2[n % 2][:],
                            in_=at_srcs[wi][128 * jt:128 * (jt + 1), :])
                        sems.inc(d, "at_wc", 16)
                    if on("sync"):
                        if n == 0:
                            sems.wait(eng, "p_st1", 16)
                            sems.wait(eng, t + "mm1", 256)
                        sems.wait(eng, "at_wc", 16 * (n + 1))
                        d = eng.dma_start_transpose(
                            slotA[:, wi * 8192:(wi + 1) * 8192]
                            .rearrange("p (c n) -> p c n", c=DC)
                            [:, :, 128 * jt:128 * (jt + 1)],
                            wstg2[n % 2][:])
                        sems.inc(d, "at_wt" + str(n % 2), 16)

                rb = [af(0 + b * D, D) for b in range(2)]
                ob = [af(2048 + b * D, D) for b in range(2)]
                for p in range(NPAIR):
                    gp = oP + p
                    if on("scalar"):
                        sems.wait(eng, t + "bm", 2 * gp - 2)
                        d = eng.dma_start(
                            out=hqB[p % 2][:],
                            in_=hq_dram[:, p * 2 * H:(p + 1) * 2 * H])
                        sems.inc(d, t + "bh" + str(p % 2), 16)
                    hqv = hqB[p % 2].rearrange("p (c n) -> p c n", c=HC)
                    for b in range(2):
                        j = 2 * p + b
                        gj = oI + j
                        if on("sync"):
                            sems.wait(eng, t + "bev", 2 * gj - 2)
                            d = eng.dma_start(
                                out=rb[j % 2][:],
                                in_=rin[128 * j:128 * (j + 1), :])
                            sems.inc(d, t + "br", 16)
                        if on("tensor"):
                            if j == 0:
                                sems.wait(eng, t + "w2p", 16 * (f + 1))
                            sems.wait(eng, t + "bh" + str(p % 2),
                                      16 * (f * 4 + p // 2 + 1))
                            sems.wait(eng, t + "bev", 2 * gj - 2)
                            pa = psum[2 * (j % 2)]
                            pb = psum[2 * (j % 2) + 1]
                            for k in range(HC):
                                lhsT = hqv[:, k, b * 128:(b + 1) * 128]
                                eng.matmul(pa[:], lhsT,
                                           W2b[:, k * D:k * D + 512],
                                           start=(k == 0),
                                           stop=(k == HC - 1))
                                mm = eng.matmul(
                                    pb[:], lhsT,
                                    W2b[:, k * D + 512:(k + 1) * D],
                                    start=(k == 0), stop=(k == HC - 1))
                            sems.inc(mm, t + "bm", 1)
                        if on("vector"):
                            sems.wait(eng, t + "bm", gj + 1)
                            sems.wait(eng, t + "br", 16 * (gj + 1))
                            sems.wait(eng, t + "bo" + str(j % 2),
                                      16 * (f * 8 + j // 2))
                            pa = psum[2 * (j % 2)]
                            pb = psum[2 * (j % 2) + 1]
                            eng.scalar_tensor_tensor(
                                ob[j % 2][:, 0:512], pa[:],
                                sc2st[:, j:j + 1],
                                rb[j % 2][:, 0:512], ALU.mult, ALU.add)
                            s2i = eng.scalar_tensor_tensor(
                                ob[j % 2][:, 512:1024], pb[:],
                                sc2st[:, j:j + 1],
                                rb[j % 2][:, 512:1024], ALU.mult, ALU.add)
                            sems.inc(s2i, t + "bev", 2)
                        if on("sync"):
                            sems.wait(eng, t + "bev", 2 * (gj + 1))
                            d = eng.dma_start(
                                out=rout[128 * j:128 * (j + 1), :],
                                in_=ob[j % 2][:])
                            sems.inc(d, t + "bo" + str(j % 2), 16)
                        if f == 0:
                            at_wprep(2 * j)
                            at_wprep(2 * j + 1)
                preB = []
                if on("sync"):
                    preB = [(t + "br", 16 * (oI + NBLK)),
                            (t + "bo0", 16 * (f * 8 + 8)),
                            (t + "bo1", 16 * (f * 8 + 8))]
                    if f == 0:
                        preB.append(("at_wt0", 16 * 16))
                        preB.append(("at_wt1", 16 * 16))
                if on("scalar"):
                    preB = [(t + "bh0", 16 * (f * 4 + 4)),
                            (t + "bh1", 16 * (f * 4 + 4))]
                if on("gpsimd") and f == 0:
                    preB = [("at_wc", 16 * 32)]
                join(t + str(f) + "B", preB)

            ffn(0)

            WK = slotA[:, 0:8192]
            WV = slotA[:, 8192:16384]
            WR = slotA[:, 16384:24576]
            WO = slotA[:, 24576:32768]
            kvb = [AB[:, silu_off + b * 2048:silu_off + (b + 1) * 2048]
                   for b in range(2)]
            scanb = [af(2048 + c * 257, 257) for c in range(DC)]
            xbA = [af(0, D), af(1024, D), af(4112, D), af(5136, D)]
            xnT = [AB[:, slotC_off + b * DC * TA:
                      slotC_off + (b + 1) * DC * TA] for b in range(2)]
            rw_bf = AB[:, slotC_off + 2 * DC * TA:slotC_off + 3 * DC * TA]
            r_bf = AB[:, slotC_off + 3 * DC * TA:slotC_off + 4 * DC * TA]
            xqb2 = [AB[:, qs_off:qs_off + D],
                    AB[:, slotC_off + 12288:slotC_off + 12288 + D]]

            if on("scalar"):
                sems.wait(eng, "p_st1", 16)
                d = eng.dma_start(out=slotB, in_=w1q_st[:, :])
                sems.inc(d, "at_w1p", 16)

            if on("vector"):
                for c in range(DC):
                    eng.memset(scanb[c][:, 0:1], 0.0)

            def at_xside(i):
                a_ = i // 2
                if on("sync"):
                    sems.wait(eng, "at_o", 16 * (i - 3))
                    d = eng.dma_start(out=xbA[i % 4][:],
                                      in_=resid1[128 * i:128 * (i + 1), :])
                    sems.inc(d, "at_x", 16)
                if on("vector"):
                    sems.wait(eng, "at_x", 16 * (i + 1))
                    op = ln_stats_v(xbA[i % 4])
                    sems.inc(op, "at_st", 1)
                    sems.wait(eng, "at_rs", i + 1)
                    eng.scalar_tensor_tensor(C["nmr"], C["m"], -1.0,
                                             C["rstd"], ALU.mult,
                                             ALU.mult)
                    sems.wait(eng, "at_qt", 16 * (i - 1))
                    eng.drain()
                    op = eng.tensor_scalar(xqb2[i % 2][:], xbA[i % 4][:],
                                           C["rstd"], C["nmr"], ALU.mult,
                                           ALU.add)
                    sems.inc(op, "at_q", 1)
                if on("scalar"):
                    sems.wait(eng, "at_st", i + 1)
                    op = rstd_s()
                    sems.inc(op, "at_rs", 1)
                if on("sync"):
                    sems.wait(eng, "at_q", i + 1)
                    sems.wait(eng, "at_rm", (a_ - 1) * DC)
                    d = eng.dma_start_transpose(
                        xnT[a_ % 2].rearrange("p (c n) -> p c n", c=DC)
                        [:, :, 128 * (i % 2):128 * (i % 2 + 1)],
                        xqb2[i % 2][:])
                    sems.inc(d, "at_qt", 16)

            at_xside(0)
            at_xside(1)
            for a in range(NA):
                if a + 1 < NA:
                    at_xside(2 * a + 2)
                    at_xside(2 * a + 3)

                for kind, Wmat, pbase in [("k", WK, 0), ("v", WV, 2),
                                          ("r", WR, 4)]:
                    for ec in range(DC):
                        n = a * DC + ec
                        if on("tensor"):
                            if kind == "k" and ec == 0 and a == 0:
                                sems.wait(eng, "at_wt0", 16 * 16)
                                sems.wait(eng, "at_wt1", 16 * 16)
                            sems.wait(eng, "at_qt", 16 * (2 * a + 2))
                            sems.wait(eng, "at_" + kind + "e", n - 1)
                            ps = psum[pbase + ec % 2]
                            for k in range(DC):
                                mm = eng.matmul(
                                    ps[:, 0:TA],
                                    Wmat[:, k * D + ec * 128:
                                         k * D + (ec + 1) * 128],
                                    xnT[a % 2][:, k * TA:(k + 1) * TA],
                                    start=(k == 0), stop=(k == DC - 1))
                            sems.inc(mm, "at_" + kind + "m", 1)
                        if kind == "k" and on("scalar"):
                            sems.wait(eng, "at_km", n + 1)
                            sems.wait(eng, "at_rw", (a - 2) * DC + ec + 1)
                            op = eng.activation(
                                kvb[a % 2][:, ec * TA:(ec + 1) * TA],
                                psum[0 + ec % 2][:, 0:TA],
                                AF.Identity, bias=0.0, scale=1.0)
                            sems.inc(op, "at_ke", 1)
                        if kind == "v" and on("vector"):
                            sems.wait(eng, "at_vm", n + 1)
                            sems.wait(eng, "at_ke", n + 1)
                            op = eng.tensor_mul(
                                kvb[a % 2][:, ec * TA:(ec + 1) * TA],
                                psum[2 + ec % 2][:, 0:TA],
                                kvb[a % 2][:, ec * TA:(ec + 1) * TA])
                            sems.inc(op, "at_ve", 1)
                        if kind == "r" and on("scalar"):
                            sems.wait(eng, "at_rm", n + 1)
                            sems.wait(eng, "at_rw", (a - 1) * DC + ec + 1)
                            op = eng.activation(
                                r_bf[:, ec * TA:(ec + 1) * TA],
                                psum[4 + ec % 2][:, 0:TA], AF.Sigmoid)
                            sems.inc(op, "at_re", 1)

                for c in range(DC):
                    n = a * DC + c
                    if on("vector"):
                        sems.wait(eng, "at_re", n + 1)
                        if c == 0:
                            sems.wait(eng, "at_om", 2 * (a - 1) + 2)
                        sl = kvb[a % 2][:, c * TA:(c + 1) * TA]
                        eng.tensor_tensor_scan(
                            scanb[c][:, 1:TA + 1],
                            decay_c[:, c:c + 1].broadcast_to([128, TA]),
                            sl, scanb[c][:, 0:1], ALU.mult, ALU.add)
                        eng.scalar_tensor_tensor(
                            sl, sl, eu_c[:, c:c + 1], scanb[c][:, 0:TA],
                            ALU.mult, ALU.add)
                        eng.tensor_copy(scanb[c][:, 0:1],
                                        scanb[c][:, TA:TA + 1])
                        op = eng.tensor_mul(rw_bf[:, c * TA:(c + 1) * TA],
                                            r_bf[:, c * TA:(c + 1) * TA], sl)
                        sems.inc(op, "at_rw", 1)

                for s_ in range(2):
                    i = 2 * a + s_
                    if on("tensor"):
                        sems.wait(eng, "at_rw", (a + 1) * DC)
                        sems.wait(eng, "at_oe", i)
                        for k in range(DC):
                            lhsT = rw_bf[:, k * TA + s_ * 128:
                                         k * TA + s_ * 128 + 128]
                            eng.matmul(psum[6][:], lhsT,
                                       WO[:, k * D:k * D + 512],
                                       start=(k == 0), stop=(k == DC - 1))
                            mm = eng.matmul(psum[7][:], lhsT,
                                            WO[:, k * D + 512:(k + 1) * D],
                                            start=(k == 0),
                                            stop=(k == DC - 1))
                        sems.inc(mm, "at_om", 1)
                    if on("vector"):
                        sems.wait(eng, "at_om", i + 1)
                        xsl = xbA[i % 4]
                        eng.tensor_add(xsl[:, 0:512], psum[6][:],
                                       xsl[:, 0:512])
                        ev = eng.tensor_add(xsl[:, 512:1024], psum[7][:],
                                            xsl[:, 512:1024])
                        sems.inc(ev, "at_oe", 1)
                    if on("gpsimd"):
                        sems.wait(eng, "at_oe", i + 1)
                        d = eng.dma_start(out=resid2[128 * i:128 * (i + 1), :],
                                          in_=xbA[i % 4][:])
                        sems.inc(d, "at_o", 16)

            pre_at = []
            if on("sync"):
                pre_at = [("at_x", 16 * NBLK), ("at_qt", 16 * NBLK)]
            if on("scalar"):
                pre_at = [("at_w1p", 16)]
            if on("gpsimd"):
                pre_at = [("at_o", 16 * NBLK)]
            join("attn", pre_at)
            ffn(1)
            if on("sync"):
                sems.wait(eng, "Fbo0", 16 * 16)
                sems.wait(eng, "Fbo1", 16 * 16)

        @block.sync
        def _(sync):
            emit("sync", sync)

        @block.scalar
        def _(scalar):
            emit("scalar", scalar)

        @block.vector
        def _(vector):
            emit("vector", vector)

        @block.gpsimd
        def _(gpsimd):
            emit("gpsimd", gpsimd)

        @block.tensor
        def _(tensor):
            emit("tensor", tensor)

        sems.check()

    mybir.codegen_inst_isa_subclasses(nc)
    return nc


_NC_CACHE = {}


def kernel(L0_attn_x, L0_attn_Wk, L0_attn_Wv, L0_attn_Wr, L0_attn_Wo,
           L0_attn_time_decay, L0_attn_time_first, L0_ffn_w1, L0_ffn_conv_w,
           L0_ffn_conv_b, L0_ffn_w2):
    op_rm, op_qs = make_ops()
    if "nc" not in _NC_CACHE:
        _NC_CACHE["nc"] = build_nc(op_rm, op_qs)
    nc = _NC_CACHE["nc"]
    x = np.ascontiguousarray(np.asarray(L0_attn_x, dtype=np.float32))
    shared = {
        "wk": np.ascontiguousarray(np.asarray(L0_attn_Wk, np.float32)),
        "wv": np.ascontiguousarray(np.asarray(L0_attn_Wv, np.float32)),
        "wr": np.ascontiguousarray(np.asarray(L0_attn_Wr, np.float32)),
        "wo": np.ascontiguousarray(np.asarray(L0_attn_Wo, np.float32)),
        "td": np.ascontiguousarray(np.asarray(L0_attn_time_decay,
                                              np.float32)),
        "tf": np.ascontiguousarray(np.asarray(L0_attn_time_first,
                                              np.float32)),
        "w1": np.ascontiguousarray(np.asarray(L0_ffn_w1, np.float32)),
        "cw": np.ascontiguousarray(np.asarray(L0_ffn_conv_w, np.float32)),
        "cb": np.ascontiguousarray(np.asarray(L0_ffn_conv_b, np.float32)),
        "w2": np.ascontiguousarray(np.asarray(L0_ffn_w2, np.float32)),
    }
    in_maps = [dict(shared, x=np.ascontiguousarray(x[b])) for b in range(8)]
    res = run_bass_kernel_spmd(nc, in_maps, core_ids=list(range(8)))
    return np.stack([r["out"] for r in res.results], axis=0)


if __name__ == "__main__":
    nc = build_nc(*make_ops())
    n_inst = sum(1 for _ in nc.all_instructions())
    print("built ok,", n_inst, "instructions")
